# version 02
# baseline (speedup 1.0000x reference)
import numpy as np
import ml_dtypes
from contextlib import ExitStack

import concourse.bass as bass
import concourse.mybir as mybir
from concourse.bass_utils import run_bass_kernel_spmd

F32 = mybir.dt.float32
BF16 = mybir.dt.bfloat16
I32 = mybir.dt.int32
AF = mybir.ActivationFunctionType
ALU = mybir.AluOpType

NCORES = 8
L = 4096
D = 4096
NHL = 64
HD = 64
GL = 4
CS = 256
NCH = 16
TH = 2048
EPS = 1e-5
MAGIC = 0x5A5A1234
NSTRIP = 72

ENGINES = ("sync", "gpsimd", "vector", "scalar", "tensor")
DEBUG_TAPS = ()


class Plan:
    def __init__(self):
        self.ops = {e: [] for e in ENGINES}
        self.cnt = {}
        self.waited = {e: {} for e in ENGINES}
        self.selfsem = {"vector": "DV", "scalar": "AC"}

    def tok(self, sem):
        return (sem, self.cnt.get(sem, 0))

    def emit(self, engine, deps, fn, inc=None, amt=1):
        deps = [d for d in deps if d is not None]
        ss = self.selfsem.get(engine)
        if ss is not None and self.cnt.get(ss, 0) > 0:
            deps.append((ss, self.cnt[ss]))
        tok = None
        if inc is not None:
            self.cnt[inc] = self.cnt.get(inc, 0) + amt
            tok = (inc, self.cnt[inc])
        needed = []
        w = self.waited[engine]
        for s, v in deps:
            if v > 0 and w.get(s, 0) < v:
                needed.append((s, v))
                w[s] = v

        def closure(eng, sems, _n=tuple(needed), _fn=fn, _tok=tok, _amt=amt):
            for s, v in _n:
                eng.wait_ge(sems[s], v)
            instr = _fn(eng, sems)
            if _tok is not None:
                if instr is None:
                    instr = eng.nop()
                instr.then_inc(sems[_tok[0]], _amt)
        self.ops[engine].append(closure)
        return tok

    def barrier(self):
        for e in ENGINES:
            self.emit(e, [], lambda eng, sems: eng.nop(), inc="BAR")
        val = self.cnt["BAR"]
        for e in ENGINES:
            self.emit(e, [("BAR", val)], lambda eng, sems: None)


def build_program(taps=()):
    nc = bass.Bass("TRN2", target_bir_lowering=False, debug=False)

    def ein(name, shape, dt=F32):
        return nc.dram_tensor(name, shape, dt, kind="ExternalInput")

    i_x = ein("x_full", [L, D])
    i_xres = ein("x_res", [TH, D])
    i_w = ein("w_strips", [NSTRIP, D, 128])
    i_wdt = ein("w_dt", [D, NHL])
    i_convw = ein("conv_w", [128, 40 * 4])
    i_convb = ein("conv_b", [128, 40])
    i_dtb = ein("dt_bias", [NHL, 1])
    i_alog = ein("a_log", [NHL, 1])
    i_dexp = ein("d_exp", [128, 32])
    i_nw = ein("norm_w", [128, 32])
    i_wout = ein("w_out", [32, 64, 128, 128])
    i_wbase = ein("w_base", [32, 32, 128, 128])
    i_baseb = ein("base_b", [128, 32])
    i_wbcat = ein("w_bcat", [32, 8, 128, 128])
    i_wa = ein("w_a", [32, 128, 64])
    i_wrt = ein("w_router", [32, 128, 16])
    i_diag = ein("diag_inj", [2, 128, 16 * CS], BF16)
    i_ident = ein("ident", [128, 128])
    i_onesb = ein("ones_bf", [128, 1], BF16)
    i_onesf = ein("ones_f", [1, 128])
    i_hsk = ein("hsk", [1, 8], I32)

    o_out = nc.dram_tensor("o_out", [TH, D], F32, kind="ExternalOutput")

    d_xnB = nc.dram_tensor("xnB", [TH, D], BF16)
    d_xC = nc.dram_tensor("xC", [D, L], BF16)
    d_bC = nc.dram_tensor("bC", [512, L], BF16)
    d_cC = nc.dram_tensor("cC", [512, L], BF16)
    d_gC = nc.dram_tensor("gC", [D, L], BF16)
    d_dtC = nc.dram_tensor("dtC", [NHL, L], F32)
    d_evec = nc.dram_tensor("evec", [NCH, NHL, CS], F32)
    d_sdvec = nc.dram_tensor("sdvec", [NCH, NHL, CS], F32)
    d_etvec = nc.dram_tensor("etvec", [NCH, NHL, HD], F32)
    d_attnC = nc.dram_tensor("attnC", [D, TH], BF16)
    d_x1C = nc.dram_tensor("x1C", [TH, D], F32)
    d_x1nC = nc.dram_tensor("x1nC", [TH, D], BF16)
    d_ffnC = nc.dram_tensor("ffnC", [D, TH], BF16)
    d_combC = nc.dram_tensor("combC", [16, TH], F32)
    d_yL = nc.dram_tensor("yL", [D, L], BF16)
    d_yL2 = nc.dram_tensor("yL2", [2 * D, TH], BF16)
    d_yC = nc.dram_tensor("yC", [2 * D, L], BF16, addr_space="Shared")
    d_flg = nc.dram_tensor("flg", [2, 8], I32, addr_space="Shared")

    scratch = {"xnB": d_xnB, "xC": d_xC, "bC": d_bC, "cC": d_cC, "gC": d_gC,
               "dtC": d_dtC, "evec": d_evec, "sdvec": d_sdvec, "attnC": d_attnC,
               "x1C": d_x1C, "x1nC": d_x1nC, "ffnC": d_ffnC, "combC": d_combC,
               "yC": d_yC}
    dbg0 = "chunk0" in taps
    taps = tuple(t for t in taps if t != "chunk0")
    if dbg0:
        t_xs0 = nc.dram_tensor("tap_xs0", [2, 128, D], BF16, kind="ExternalOutput")
        t_dtT0 = nc.dram_tensor("tap_dtT0", [128, 2 * NHL], F32, kind="ExternalOutput")
        t_dtf = nc.dram_tensor("tap_dtf", [NHL, L], F32, kind="ExternalOutput")
        t_gml0 = nc.dram_tensor("tap_gml0", [2, 128, 16 * CS], BF16, kind="ExternalOutput")
        t_csd0 = nc.dram_tensor("tap_csd0", [128, 16 * CS], BF16, kind="ExternalOutput")
        t_ysb0 = nc.dram_tensor("tap_ysb0", [128, 32 * CS], BF16, kind="ExternalOutput")
        t_gmt0 = nc.dram_tensor("tap_gmt0", [128, GL * 512], BF16, kind="ExternalOutput")
    tap_outs = {}
    for t in taps:
        src = scratch[t]
        tap_outs[t] = nc.dram_tensor("tap_" + t, list(src.shape), src.dtype,
                                     kind="ExternalOutput")

    P = Plan()
    ctx = ExitStack()
    with ctx:
        sem_names = ["XD", "GD", "HS", "PE", "DV", "AC", "GC", "BAR"]
        sems = {nm: ctx.enter_context(nc.semaphore(nm)) for nm in sem_names}

        c_ident = ctx.enter_context(nc.sbuf_tensor("c_ident", [128, 128], F32))
        c_onesb = ctx.enter_context(nc.sbuf_tensor("c_onesb", [128, 1], BF16))
        c_onesf = ctx.enter_context(nc.sbuf_tensor("c_onesf", [1, 128], F32))
        c_convw = ctx.enter_context(nc.sbuf_tensor("c_convw", [128, 40 * 4], F32))
        c_convb = ctx.enter_context(nc.sbuf_tensor("c_convb", [128, 40], F32))
        c_dtb = ctx.enter_context(nc.sbuf_tensor("c_dtb", [NHL, 1], F32))
        c_alog = ctx.enter_context(nc.sbuf_tensor("c_alog", [NHL, 1], F32))
        c_dexp = ctx.enter_context(nc.sbuf_tensor("c_dexp", [128, 32], F32))
        c_nw = ctx.enter_context(nc.sbuf_tensor("c_nw", [128, 32], F32))
        c_bb = ctx.enter_context(nc.sbuf_tensor("c_bb", [128, 32], F32))
        c_hsk = ctx.enter_context(nc.sbuf_tensor("c_hsk", [1, 8], I32))
        c_magic = ctx.enter_context(nc.sbuf_tensor("c_magic", [1, 2], I32))
        c_flagin = ctx.enter_context(nc.sbuf_tensor("c_flagin", [1, 1], I32))
        c_consts = ctx.enter_context(nc.sbuf_tensor("c_consts", [128, 8], F32))

        def dma(engine, semn, out_ap, in_ap, deps=(), transpose=False):
            def fn(eng, sems, _o=out_ap, _i=in_ap, _t=transpose):
                return eng.dma_start(out=_o, in_=_i, transpose=_t)
            return P.emit(engine, deps, fn, inc=semn, amt=16)

        def op(engine, semn, fn, deps=()):
            return P.emit(engine, deps, fn, inc=semn)

        def mm(fn, deps=(), tok=False):
            return P.emit("tensor", deps, fn, inc="PE" if tok else None)

        def ln_stats(src_ap_fn, bstT, statT, scol, deps):
            t = None
            for i in range(8):
                t = op("vector", "DV",
                       lambda eng, sems, ii=i: eng.bn_stats(
                           bstT[:, ii * 6:(ii + 1) * 6],
                           src_ap_fn(ii)), deps=deps if i == 0 else [])
            return op("vector", "DV", lambda eng, sems: eng.bn_aggr(
                statT[:, scol:scol + 2], bstT[:, :]), deps=[t])

        for dst, src in [
            (c_ident[:, :], i_ident[:, :]),
            (c_onesb[:, :], i_onesb[:, :]),
            (c_onesf[:, :], i_onesf[:, :]),
            (c_convw[:, :], i_convw[:, :]),
            (c_convb[:, :], i_convb[:, :]),
            (c_dtb[:, :], i_dtb[:, :]),
            (c_alog[:, :], i_alog[:, :]),
            (c_dexp[:, :], i_dexp[:, :]),
            (c_nw[:, :], i_nw[:, :]),
            (c_bb[:, :], i_bb_in[:, :]) if False else (c_bb[:, :], i_baseb[:, :]),
            (c_hsk[:, :], i_hsk[:, :]),
        ]:
            dma("sync", "XD", dst, src)
        consts_ready = P.tok("XD")

        CONSTVALS = [EPS, -0.5, -1.0, 1.0 / 1024.0, 0.5, 1.0, 1e-6, 0.0]
        cv_toks = []
        for ci, cv in enumerate(CONSTVALS):
            cv_toks.append(op("vector", "DV",
                              lambda eng, sems, i=ci, v=cv: eng.memset(
                                  c_consts[:, i:i + 1], v)))
            nc.const_aps.register(F32, cv, c_consts[:, ci:ci + 1])
        const_vals_tok = cv_toks[-1]

        gp_regs = {}

        def gp_setup(eng, sems):
            for nm in ["yw", "yr", "myf", "otf", "fval", "cond", "hcnt", "tmp"]:
                gp_regs[nm] = eng.register(nm).__enter__()
            eng.memset(c_magic[:1, 0:1], MAGIC)
            eng.memset(c_magic[:1, 1:2], 0)
            eng.wait_ge(sems["XD"], consts_ready[1])
            eng.reg_load(gp_regs["yw"], c_hsk[:1, 0:1])
            eng.reg_load(gp_regs["yr"], c_hsk[:1, 1:2])
            eng.reg_load(gp_regs["myf"], c_hsk[:1, 2:3])
            eng.reg_load(gp_regs["otf"], c_hsk[:1, 3:4])
            eng.reg_mov(gp_regs["hcnt"], 0)
            return None
        P.emit("gpsimd", [], gp_setup, inc="GC")
        gp_ready = P.tok("GC")

        p0 = ExitStack()
        xbuf = p0.enter_context(nc.sbuf_tensor("xbuf", [128, D], F32))
        stat = p0.enter_context(nc.sbuf_tensor("stat", [128, 8], F32))
        bst = p0.enter_context(nc.sbuf_tensor("bst", [128, 6 * 8], F32))
        xn_tm = p0.enter_context(nc.sbuf_tensor("xn_tm", [128, D], BF16))
        xn_fm = p0.enter_context(nc.sbuf_tensor("xn_fm", [128, 32 * TH], BF16))
        wbuf = [p0.enter_context(nc.sbuf_tensor(f"wbuf{i}", [128, 32 * 128], BF16))
                for i in range(2)]
        raw = [p0.enter_context(nc.sbuf_tensor(f"raw{i}", [128, 3 + TH], BF16))
               for i in range(2)]
        cacc = p0.enter_context(nc.sbuf_tensor("cacc", [128, TH], F32))
        etile = p0.enter_context(nc.sbuf_tensor("etile", [128, TH], BF16))
        carry = p0.enter_context(nc.sbuf_tensor("carry", [128, 40 * 3], BF16))
        dtstg = p0.enter_context(nc.sbuf_tensor("dtstg", [NHL, 512], F32))
        ps0 = [p0.enter_context(nc.psum_tensor(f"ps0_{i}", [128, 512], F32))
               for i in range(8)]

        carry_tok = op("vector", "DV",
                       lambda eng, sems: eng.memset(carry[:, :], 0.0))

        bank_free = [None] * 8
        wbuf_free = [None, None]
        raw_free = [None, None]
        etile_free = None
        cacc_free = None
        dtstg_free = None
        carry_w = {}
        xbuf_free = None
        xn_tm_free = None
        pass_last_mm = None
        prev_tr_done = None

        for pp in range(2):
            t0 = pp * TH
            st_prev = prev_tr_done
            for tt in range(16):
                ld = dma("sync", "XD", xbuf[:, :],
                         i_x[t0 + tt * 128: t0 + (tt + 1) * 128, :],
                         deps=[xbuf_free])
                s2 = ln_stats(lambda ii: xbuf[:, ii * 512:(ii + 1) * 512],
                              bst, stat, 0, [ld])
                s6 = op("scalar", "AC", lambda eng, sems: eng.activation(
                    stat[:, 2:3], stat[:, 1:2], AF.Ln, bias=EPS), deps=[s2])
                s7 = op("scalar", "AC", lambda eng, sems: eng.activation(
                    stat[:, 3:4], stat[:, 2:3], AF.Exp, scale=-0.5), deps=[s6])
                s8 = op("vector", "DV", lambda eng, sems: eng.tensor_scalar(
                    xn_tm[:, :], xbuf[:, :], stat[:, 0:1], stat[:, 3:4],
                    ALU.subtract, ALU.mult), deps=[s7, xn_tm_free])
                xbuf_free = s8
                st = dma("sync", "XD", d_xnB[tt * 128:(tt + 1) * 128, :],
                         xn_tm[:, :], deps=[s8, st_prev])
                xn_tm_free = st
                st_prev = None
            ln_done = P.tok("XD")

            for kb in range(32):
                dma("sync", "XD", xn_fm[:, kb * TH:(kb + 1) * TH],
                    d_xnB[:, kb * 128:(kb + 1) * 128],
                    deps=[ln_done, pass_last_mm], transpose=True)
            xn_ready = P.tok("XD")
            prev_tr_done = xn_ready

            for s in range(NSTRIP + 1):
                is_dt = (s == NSTRIP)
                par = s % 2
                if is_dt:
                    wl = dma("gpsimd", "GD",
                             wbuf[par][:, 0:32 * 64].rearrange("p (k m) -> p k m", m=64),
                             i_wdt[:, :].rearrange("(k p) m -> p k m", p=128),
                             deps=[gp_ready, wbuf_free[par]])
                    MM = 64
                else:
                    wl = dma("gpsimd", "GD",
                             wbuf[par][:, :].rearrange("p (k m) -> p k m", m=128),
                             i_w[s, :, :].rearrange("(k p) m -> p k m", p=128),
                             deps=[gp_ready, wbuf_free[par]])
                    MM = 128
                wb = wbuf[par]
                is_conv = (not is_dt) and s < 40
                is_gate = (not is_dt) and s >= 40

                evac_toks = []
                last_mm = None
                for tt in range(4):
                    bank = (s % 2) * 4 + tt
                    for k in range(32):
                        last_mm = mm(
                            lambda eng, sems, b=bank, kk=k, t=tt, w=wb, m=MM: eng.matmul(
                                ps0[b][0:m, 0:512],
                                w[:, kk * m:(kk + 1) * m],
                                xn_fm[:, kk * TH + t * 512:kk * TH + (t + 1) * 512],
                                start=(kk == 0), stop=(kk == 31)),
                            deps=([wl, xn_ready, bank_free[bank]] if k == 0 else []),
                            tok=(k == 31))
                    if is_conv:
                        ev = op("vector", "DV",
                                lambda eng, sems, b=bank, p=par, t=tt: eng.tensor_copy(
                                    raw[p][:, 3 + t * 512:3 + (t + 1) * 512],
                                    ps0[b][:, 0:512]),
                                deps=[last_mm] + ([raw_free[par]] if tt == 0 else []))
                    elif is_gate:
                        ev = op("scalar", "AC",
                                lambda eng, sems, b=bank, t=tt: eng.activation(
                                    etile[:, t * 512:(t + 1) * 512],
                                    ps0[b][:, 0:512], AF.Silu),
                                deps=[last_mm] + ([etile_free] if tt == 0 else []))
                    else:
                        ev = op("scalar", "AC",
                                lambda eng, sems, b=bank: eng.copy(
                                    dtstg[:, :], ps0[b][0:NHL, 0:512]),
                                deps=[last_mm, dtstg_free])
                        dtstg_free = dma("sync", "XD",
                                         d_dtC[:, t0 + tt * 512:t0 + (tt + 1) * 512],
                                         dtstg[:, :], deps=[ev])
                    bank_free[bank] = ev
                    evac_toks.append(ev)
                wbuf_free[par] = last_mm
                pass_last_mm = last_mm

                if is_conv:
                    cc = op("vector", "DV",
                            lambda eng, sems, p=par, ss=s: eng.tensor_copy(
                                raw[p][:, 0:3], carry[:, ss * 3:(ss + 1) * 3]),
                            deps=[carry_tok if pp == 0 else carry_w.get(s),
                                  evac_toks[0]])
                    sc = op("vector", "DV",
                            lambda eng, sems, p=par, ss=s: eng.tensor_copy(
                                carry[:, ss * 3:(ss + 1) * 3], raw[p][:, TH:TH + 3]),
                            deps=[evac_toks[-1], cc])
                    carry_w[s] = sc
                    t1 = op("vector", "DV",
                            lambda eng, sems, p=par, ss=s: eng.tensor_scalar(
                                cacc[:, :], raw[p][:, 0:TH],
                                c_convw[:, ss * 4:ss * 4 + 1],
                                c_convb[:, ss:ss + 1], ALU.mult, ALU.add),
                            deps=[cc, evac_toks[-1], cacc_free])
                    for k in range(1, 4):
                        t1 = op("vector", "DV",
                                lambda eng, sems, p=par, ss=s, kk=k: eng.scalar_tensor_tensor(
                                    cacc[:, :], raw[p][:, kk:kk + TH],
                                    c_convw[:, ss * 4 + kk:ss * 4 + kk + 1],
                                    cacc[:, :], ALU.mult, ALU.add), deps=[t1])
                    raw_free[par] = t1
                    sl = op("scalar", "AC", lambda eng, sems: eng.activation(
                        etile[:, :], cacc[:, :], AF.Silu), deps=[t1, etile_free])
                    cacc_free = sl
                    if s < 32:
                        dst = d_xC[s * 128:(s + 1) * 128, t0:t0 + TH]
                    elif s < 36:
                        dst = d_bC[(s - 32) * 128:(s - 31) * 128, t0:t0 + TH]
                    else:
                        dst = d_cC[(s - 36) * 128:(s - 35) * 128, t0:t0 + TH]
                    st = dma("sync", "XD", dst, etile[:, :], deps=[sl])
                    etile_free = st
                elif is_gate:
                    st = dma("sync", "XD",
                             d_gC[(s - 40) * 128:(s - 39) * 128, t0:t0 + TH],
                             etile[:, :], deps=[evac_toks[-1]])
                    etile_free = st

        P.barrier()
        p0.close()

        p1 = ExitStack()
        t_dt = p1.enter_context(nc.sbuf_tensor("t_dt", [NHL, L], F32))
        t_AC = p1.enter_context(nc.sbuf_tensor("t_AC", [NHL, L], F32))
        t_ch = [p1.enter_context(nc.sbuf_tensor(f"t_ch{i}", [NHL, CS], F32))
                for i in range(3)]
        t_a = p1.enter_context(nc.sbuf_tensor("t_a", [NHL, 1], F32))
        t_dtT = p1.enter_context(nc.sbuf_tensor("t_dtT", [128, 2 * NHL], F32))
        t_decT = p1.enter_context(nc.sbuf_tensor("t_decT", [128, 2 * NHL], F32))
        t_dec = p1.enter_context(nc.sbuf_tensor("t_dec", [NHL, CS], F32))
        xsT = [p1.enter_context(nc.sbuf_tensor(f"xsT{j}", [128, D], BF16))
               for j in range(2)]
        bT = [p1.enter_context(nc.sbuf_tensor(f"bT{j}", [128, 512], BF16))
              for j in range(2)]
        BdT = p1.enter_context(nc.sbuf_tensor("BdT", [128, 2 * 16 * 128], BF16))
        bC_sb = p1.enter_context(nc.sbuf_tensor("bC_sb", [128, GL * CS], BF16))
        cC_sb = p1.enter_context(nc.sbuf_tensor("cC_sb", [128, GL * CS], BF16))
        gC_sb = p1.enter_context(nc.sbuf_tensor("gC_sb", [128, 8 * CS], BF16))
        xf_sb = p1.enter_context(nc.sbuf_tensor("xf_sb", [128, 8 * CS], BF16))
        S_acc = p1.enter_context(nc.sbuf_tensor("S_acc", [128, NHL * HD], F32))
        S_bf = p1.enter_context(nc.sbuf_tensor("S_bf", [128, NHL * HD], BF16))
        Sdec = p1.enter_context(nc.sbuf_tensor("Sdec", [128, NHL * HD], F32))
        cs_sb = p1.enter_context(nc.sbuf_tensor("cs_sb", [128, NHL * HD], BF16))
        d0_sb = p1.enter_context(nc.sbuf_tensor("d0_sb", [128, 8 * CS], F32))
        sd_sb = p1.enter_context(nc.sbuf_tensor("sd_sb", [128, 16 * CS], BF16))
        Csd_sb = p1.enter_context(nc.sbuf_tensor("Csd_sb", [128, 16 * CS], BF16))
        gml = [p1.enter_context(nc.sbuf_tensor(f"gml{j}", [128, 16 * CS], BF16))
               for j in range(2)]
        gmt_sb = p1.enter_context(nc.sbuf_tensor("gmt_sb", [128, GL * 512], BF16))
        Y_sb = p1.enter_context(nc.sbuf_tensor("Y_sb", [128, 32 * CS], BF16))
        ysq_sb = [p1.enter_context(nc.sbuf_tensor(f"ysq{i}", [128, CS], BF16))
                  for i in range(2)]
        rstd_row = p1.enter_context(nc.sbuf_tensor("rstd_row", [1, CS], F32))

        ps_m3 = [p1.enter_context(nc.psum_tensor(f"ps_m3_{i}", [128, 512], F32))
                 for i in range(2)]
        ps_gm = p1.enter_context(nc.psum_tensor("ps_gm", [128, 512], F32))
        ps_y = [p1.enter_context(nc.psum_tensor(f"ps_y_{i}", [128, 512], F32))
                for i in range(2)]
        ps_gn = p1.enter_context(nc.psum_tensor("ps_gn", [1, 512], F32))
        ps_rep = p1.enter_context(nc.psum_tensor("ps_rep", [128, 512], F32))
        ps_tr = p1.enter_context(nc.psum_tensor("ps_tr", [128, 512], F32))

        c_diag = p1.enter_context(nc.sbuf_tensor("c_diag", [128, 2 * 16 * CS], BF16))
        diag_ld = dma("sync", "XD", c_diag[:, :].rearrange("p (a b) -> p a b", a=2),
                      i_diag[:, :, :].transpose([1, 0, 2]))

        sinit = op("vector", "DV", lambda eng, sems: eng.memset(S_acc[:, :], 0.0))

        dt_ld = dma("sync", "XD", t_dt[:, :], d_dtC[:, :])
        e1 = op("scalar", "AC", lambda eng, sems: eng.activation(
            t_dt[:, :], t_dt[:, :], AF.Exp, bias=c_dtb[:, 0:1]),
            deps=[dt_ld, consts_ready])
        e2 = op("vector", "DV", lambda eng, sems: eng.tensor_scalar_add(
            t_dt[:, :], t_dt[:, :], 1.0), deps=[e1])
        e3 = op("scalar", "AC", lambda eng, sems: eng.activation(
            t_dt[:, :], t_dt[:, :], AF.Ln), deps=[e2])
        a1 = op("scalar", "AC", lambda eng, sems: eng.activation(
            t_a[:, :], c_alog[:, :], AF.Exp), deps=[consts_ready])
        a2 = op("vector", "DV", lambda eng, sems: eng.tensor_scalar_mul(
            t_a[:, :], t_a[:, :], -1.0), deps=[a1])
        ev_sts = []
        sd_sts = []
        ch_dma_free = None
        for z in range(NCH):
            zc = z * CS
            adz = op("vector", "DV", lambda eng, sems, c=zc: eng.tensor_scalar_mul(
                t_ch[0][:, :], t_dt[:, c:c + CS], t_a[:, 0:1]), deps=[e3, a2])
            eaz = op("scalar", "AC", lambda eng, sems: eng.activation(
                t_ch[2][:, :], t_ch[0][:, :], AF.Exp), deps=[adz, ch_dma_free])
            ezz = op("vector", "DV", lambda eng, sems: eng.memset(
                t_ch[2][:, 0:1], 0.0), deps=[eaz])
            evs = dma("sync", "XD", d_evec[z, :, :], t_ch[2][:, :], deps=[ezz])
            ev_sts.append(evs)
            cum = adz
            cu, ot = 0, 1
            shift = 1
            while shift < CS:
                sh = shift
                last = (sh * 2 >= CS)
                dstT = t_AC if last else t_ch[ot]
                doff = zc if last else 0
                c1 = op("vector", "DV",
                        lambda eng, sems, a_=cu, d_=dstT, o_=doff, s=sh: eng.tensor_add(
                            bass.AP(d_, o_ + s, [[L if d_ is t_AC else CS, NHL], [1, CS - s]]),
                            bass.AP(t_ch[a_], s, [[CS, NHL], [1, CS - s]]),
                            bass.AP(t_ch[a_], 0, [[CS, NHL], [1, CS - s]])), deps=[cum])
                cum = op("vector", "DV",
                         lambda eng, sems, a_=cu, d_=dstT, o_=doff, s=sh: eng.tensor_copy(
                             bass.AP(d_, o_, [[L if d_ is t_AC else CS, NHL], [1, s]]),
                             bass.AP(t_ch[a_], 0, [[CS, NHL], [1, s]])), deps=[c1])
                cu, ot = ot, cu
                shift *= 2
            sdz = op("scalar", "AC", lambda eng, sems, c=zc: eng.activation(
                t_ch[2][:, :], t_AC[:, c:c + CS], AF.Exp), deps=[cum, evs])
            sds = dma("sync", "XD", d_sdvec[z, :, :], t_ch[2][:, :], deps=[sdz])
            sd_sts.append(sds)
            etx = op("vector", "DV", lambda eng, sems: eng.tensor_copy(
                t_ch[1][:, 0:HD],
                bass.AP(t_ch[2], CS - 1, [[CS, NHL], [0, HD]])), deps=[sdz])
            ets = dma("sync", "XD", d_etvec[z, :, :], t_ch[1][:, 0:HD], deps=[etx])
            sd_sts.append(ets)
            ch_dma_free = ets
        ac_done = P.tok("DV")
        ev_st = ev_sts[-1]
        sd_st = sd_sts[-1]
        t_ACf = t_AC

        y_store_toks = []
        m3bank_free = [None, None]
        ybank_free = [None, None]
        ysq_free = [None, None]
        gn_free = None
        rrep_prev = None
        chunk_free = None
        tr_free = None
        prev_supd = sinit
        prev_m4 = None

        for z in range(NCH):
            zc = z * CS
            rdeps = [chunk_free]
            trx = [dma("sync", "XD", xsT[j][:, :],
                       d_xC[:, zc + j * 128:zc + (j + 1) * 128],
                       deps=rdeps, transpose=True) for j in range(2)]
            trb = [dma("sync", "XD", bT[j][:, :],
                       d_bC[:, zc + j * 128:zc + (j + 1) * 128],
                       deps=rdeps, transpose=True) for j in range(2)]
            rb = dma("sync", "XD", bC_sb[:, :].rearrange("p (g t) -> p g t", t=CS),
                     d_bC[:, zc:zc + CS].rearrange("(g p) t -> p g t", p=128),
                     deps=rdeps)
            rc = dma("sync", "XD", cC_sb[:, :].rearrange("p (g t) -> p g t", t=CS),
                     d_cC[:, zc:zc + CS].rearrange("(g p) t -> p g t", p=128),
                     deps=rdeps)
            bc_sd = [None] * GL
            bc_et = dma("gpsimd", "GD", Sdec[:, :],
                        bass.AP(d_etvec, z * NHL * HD, [[0, 128], [1, NHL * HD]]),
                        deps=[sd_st, gp_ready, chunk_free])

            dc = op("scalar", "AC", lambda eng, sems, a=t_ACf, c=zc: eng.activation(
                t_dec[:, :], a[:, c:c + CS], AF.Exp, scale=-1.0,
                bias=bass.AP(a, c + CS - 1, [[L, NHL], [1, 1]])),
                deps=[ac_done, tr_free, chunk_free])
            trtk = []
            tr_chain = tr_free
            for j in range(2):
                tdec = mm(lambda eng, sems, jj=j: eng.transpose(
                    ps_tr[:, 0:NHL], t_dec[:, jj * 128:(jj + 1) * 128],
                    c_ident[0:NHL, 0:NHL]),
                    deps=[dc, consts_ready, tr_chain, chunk_free], tok=True)
                edec = op("scalar", "AC", lambda eng, sems, jj=j: eng.copy(
                    t_decT[:, jj * NHL:(jj + 1) * NHL], ps_tr[:, 0:NHL]),
                    deps=[tdec])
                tdt = mm(lambda eng, sems, jj=j, c=zc: eng.transpose(
                    ps_tr[:, 0:NHL], t_dt[:, c + jj * 128:c + (jj + 1) * 128],
                    c_ident[0:NHL, 0:NHL]), deps=[e3, edec], tok=True)
                edt = op("scalar", "AC", lambda eng, sems, jj=j: eng.copy(
                    t_dtT[:, jj * NHL:(jj + 1) * NHL], ps_tr[:, 0:NHL]),
                    deps=[tdt])
                tr_chain = edt
                trtk.append(edt)
            tr_free = trtk[-1]
            trdone = trtk[-1]

            xs_toks = []
            for j in range(2):
                xk = op("vector", "DV", lambda eng, sems, jj=j: eng.tensor_mul(
                    xsT[jj][:, :].rearrange("p (h d) -> p h d", d=HD),
                    xsT[jj][:, :].rearrange("p (h d) -> p h d", d=HD),
                    bass.AP(t_dtT, jj * NHL, [[2 * NHL, 128], [1, NHL], [0, HD]])),
                    deps=[trx[j], trdone])
                xs_toks.append(xk)
            if dbg0 and z == 0:
                dma("sync", "XD", t_dtT0[:, :], t_dtT[:, :], deps=[trdone])
                dma("sync", "XD", t_dtf[:, :], t_dt[:, :], deps=[e3])
                for j in range(2):
                    dma("sync", "XD", t_xs0[j, :, :], xsT[j][:, :],
                        deps=[xs_toks[j]])

            sbf = op("vector", "DV", lambda eng, sems: eng.tensor_copy(
                S_bf[:, :], S_acc[:, :]), deps=[prev_supd, prev_m4, chunk_free])

            cs_evacs = []
            bd_prev = None
            for g in range(GL):
                bdg = []
                for j in range(2):
                    bk = op("vector", "DV", lambda eng, sems, jj=j, gg=g: eng.tensor_mul(
                        bass.AP(BdT, jj * 2048, [[2 * 2048, 128], [128, 16], [1, 128]]),
                        bass.AP(bT[jj], gg * 128, [[512, 128], [0, 16], [1, 128]]),
                        bass.AP(t_decT, jj * NHL + 16 * gg, [[2 * NHL, 128], [1, 16], [0, 128]])),
                        deps=[trb[j], trdone, bd_prev])
                    bdg.append(bk)
                bankp = g % 2
                mm_last = None
                for h16 in range(16):
                    h = g * 16 + h16
                    col = (h16 % 8) * 64
                    for j in range(2):
                        k0deps = [bdg[j], xs_toks[j]]
                        if h16 in (0, 8) and j == 0:
                            k0deps.append(m3bank_free[bankp])
                        mm_last = mm(
                            lambda eng, sems, jj=j, hh=h, h1=h16, c=col, bp=bankp: eng.matmul(
                                ps_m3[bp][:, c:c + 64],
                                bass.AP(BdT, jj * 2048 + h1 * 128,
                                        [[2 * 2048, 128], [1, 128]]),
                                xsT[jj][:, hh * 64:(hh + 1) * 64],
                                start=(jj == 0), stop=(jj == 1)),
                            deps=k0deps, tok=(j == 1 and h16 % 8 == 7))
                    if h16 % 8 == 7:
                        evc = op("vector", "DV",
                                 lambda eng, sems, hh=h, bp=bankp: eng.tensor_copy(
                                     cs_sb[:, (hh - 7) * 64:(hh + 1) * 64],
                                     ps_m3[bp][:, 0:512]), deps=[mm_last])
                        m3bank_free[bankp] = evc
                        cs_evacs.append(evc)
                bd_prev = cs_evacs[-1]
            cs_done = cs_evacs[-1]

            gm_evacs = []
            for g in range(GL):
                mms = []
                for j in range(2):
                    t = mm(lambda eng, sems, jj=j, gg=g: eng.matmul(
                        ps_gm[:, jj * 256:(jj + 1) * 256],
                        bC_sb[:, gg * CS + jj * 128:gg * CS + (jj + 1) * 128],
                        cC_sb[:, gg * CS:(gg + 1) * CS],
                        start=True, stop=True),
                        deps=[rb, rc] + ([gm_evacs[-1]] if g > 0 else []),
                        tok=True)
                    mms.append(t)
                ev = op("vector", "DV", lambda eng, sems, gg=g: eng.tensor_copy(
                    gmt_sb[:, gg * 512:(gg + 1) * 512], ps_gm[:, 0:512]), deps=mms)
                gm_evacs.append(ev)

            su1 = op("vector", "DV", lambda eng, sems: eng.tensor_mul(
                S_acc[:, :], S_acc[:, :], Sdec[:, :]), deps=[sbf, bc_et, prev_supd])
            su2 = op("vector", "DV", lambda eng, sems: eng.tensor_add(
                S_acc[:, :], S_acc[:, :], cs_sb[:, :]), deps=[su1, cs_done])
            prev_supd = su2

            m4_toks = []
            prev_pe_y = None
            for g in range(GL):
                bc_sd[g] = dma("gpsimd", "GD", sd_sb[:, :],
                               bass.AP(d_sdvec, (z * NHL + 16 * g) * CS,
                                       [[0, 128], [1, 16 * CS]]),
                               deps=[sd_st, chunk_free,
                                     P.tok("DV") if g > 0 else None])
                rg = dma("sync", "XD",
                         gC_sb[:, :].rearrange("p (s t) -> p s t", t=CS),
                         d_gC[g * 1024:(g + 1) * 1024, zc:zc + CS].rearrange(
                             "(s p) t -> p s t", p=128),
                         deps=[chunk_free, P.tok("DV") if g > 0 else None])
                rx = dma("sync", "XD",
                         xf_sb[:, :].rearrange("p (s t) -> p s t", t=CS),
                         d_xC[g * 1024:(g + 1) * 1024, zc:zc + CS].rearrange(
                             "(s p) t -> p s t", p=128),
                         deps=[chunk_free, P.tok("DV") if g > 0 else None])
                csd = op("vector", "DV", lambda eng, sems, gg=g: eng.tensor_mul(
                    Csd_sb[:, :].rearrange("p (h t) -> p h t", t=CS),
                    bass.AP(cC_sb, gg * CS, [[GL * CS, 128], [0, 16], [1, CS]]),
                    sd_sb[:, :].rearrange("p (h t) -> p h t", t=CS)),
                    deps=[rc, bc_sd[g], prev_pe_y])
                scn = []
                for j in range(2):
                    gm2 = None
                    for hh in range(2):
                        bcd = dma("gpsimd", "GD", d0_sb[:, :],
                                  bass.AP(d_evec, (z * NHL + 16 * g + 8 * hh) * CS,
                                          [[0, 128], [1, 8 * CS]]),
                                  deps=[ev_st, gp_ready, chunk_free, P.tok("DV")])
                        sc1 = op("vector", "DV",
                                 lambda eng, sems, jj=j, h2=hh: eng.tensor_tensor_scan(
                                     gml[jj][:, h2 * 8 * CS:(h2 + 1) * 8 * CS],
                                     d0_sb[:, :],
                                     c_diag[:, jj * 16 * CS + h2 * 8 * CS:
                                            jj * 16 * CS + (h2 + 1) * 8 * CS], 0.0,
                                     ALU.mult, ALU.max), deps=[bcd, diag_ld, prev_pe_y])
                        gm2 = op("vector", "DV",
                                 lambda eng, sems, jj=j, gg=g, h2=hh: eng.tensor_mul(
                                     gml[jj][:, h2 * 8 * CS:(h2 + 1) * 8 * CS]
                                     .rearrange("p (h t) -> p h t", t=CS),
                                     gml[jj][:, h2 * 8 * CS:(h2 + 1) * 8 * CS]
                                     .rearrange("p (h t) -> p h t", t=CS),
                                     bass.AP(gmt_sb, gg * 512 + jj * 256,
                                             [[GL * 512, 128], [0, 8], [1, CS]])),
                                 deps=[sc1, gm_evacs[g]])
                    scn.append(gm2)
                if dbg0 and z == 0 and g == 0:
                    for j in range(2):
                        dma("sync", "XD", t_gml0[j, :, :], gml[j][:, :],
                            deps=[scn[j]])
                    dma("sync", "XD", t_csd0[:, :], Csd_sb[:, :], deps=[csd])
                for m in range(8):
                    hp = g * 16 + 2 * m
                    bankp = m % 2
                    mm_toks = []
                    for hh2 in range(2):
                        h = hp + hh2
                        h16 = h - g * 16
                        b0 = hh2 * 64
                        tp = (0, hh2 * 64)
                        d0d = [scn[0], xs_toks[0]]
                        if hh2 == 0:
                            d0d.append(ybank_free[bankp])
                        mm(lambda eng, sems, hl=h, b=b0, t=tp, h1=h16, bp=bankp: eng.matmul(
                            ps_y[bp][b:b + 64, 0:256],
                            xsT[0][:, hl * 64:(hl + 1) * 64],
                            gml[0][:, h1 * CS:(h1 + 1) * CS],
                            start=True, stop=False, tile_position=t), deps=d0d)
                        mm(lambda eng, sems, hl=h, b=b0, t=tp, h1=h16, bp=bankp: eng.matmul(
                            ps_y[bp][b:b + 64, 0:256],
                            xsT[1][:, hl * 64:(hl + 1) * 64],
                            gml[1][:, h1 * CS:(h1 + 1) * CS],
                            start=False, stop=False, tile_position=t),
                           deps=[scn[1], xs_toks[1]])
                        t3 = mm(lambda eng, sems, hl=h, b=b0, t=tp, h1=h16, bp=bankp: eng.matmul(
                            ps_y[bp][b:b + 64, 0:256],
                            S_bf[:, hl * 64:(hl + 1) * 64],
                            Csd_sb[:, h1 * CS:(h1 + 1) * CS],
                            start=False, stop=True, tile_position=t),
                            deps=[sbf, csd], tok=True)
                        mm_toks.append(t3)
                    m4_toks.append(mm_toks[-1])
                    prev_pe_y = mm_toks[-1]
                    strip = hp // 2
                    sg = strip % 8
                    evy = op("vector", "DV",
                             lambda eng, sems, st=strip, s8=sg, bp=bankp: eng.scalar_tensor_tensor(
                                 Y_sb[:, st * CS:(st + 1) * CS],
                                 xf_sb[:, s8 * CS:(s8 + 1) * CS],
                                 c_dexp[:, st:st + 1],
                                 ps_y[bp][:, 0:256], ALU.mult, ALU.add),
                             deps=mm_toks + [rx, consts_ready, chunk_free])
                    ybank_free[bankp] = evy
                    gt = op("vector", "DV", lambda eng, sems, st=strip, s8=sg: eng.tensor_mul(
                        Y_sb[:, st * CS:(st + 1) * CS],
                        Y_sb[:, st * CS:(st + 1) * CS],
                        gC_sb[:, s8 * CS:(s8 + 1) * CS]), deps=[evy, rg])
                    sq = op("scalar", "AC", lambda eng, sems, st=strip, pq=m % 2: eng.activation(
                        ysq_sb[pq][:, :], Y_sb[:, st * CS:(st + 1) * CS], AF.Square),
                        deps=[gt, ysq_free[m % 2]])
                    gsum = mm(lambda eng, sems, gg=g, mi=m, pq=m % 2: eng.matmul(
                        ps_gn[0:1, (gg % 2) * 256:(gg % 2) * 256 + 256],
                        c_onesb[:, 0:1], ysq_sb[pq][:, :],
                        start=(mi == 0), stop=(mi == 7)),
                        deps=[sq, consts_ready] + ([gn_free] if m == 0 else []),
                        tok=(m == 7))
                    ysq_free[m % 2] = gsum
                    if m == 7:
                        last_gsum = gsum
                r1 = op("scalar", "AC", lambda eng, sems, gg=g: eng.activation(
                    rstd_row[:, :], ps_gn[0:1, (gg % 2) * 256:(gg % 2) * 256 + 256],
                    AF.Ln, scale=1.0 / 1024.0, bias=EPS),
                    deps=[last_gsum, rrep_prev])
                gn_free = r1
                r2 = op("scalar", "AC", lambda eng, sems: eng.activation(
                    rstd_row[:, :], rstd_row[:, :], AF.Exp, scale=-0.5), deps=[r1])
                rrep = mm(lambda eng, sems: eng.matmul(
                    ps_rep[:, 0:256], c_onesf[:, :], rstd_row[:, :],
                    start=True, stop=True),
                    deps=[r2, consts_ready, P.tok("DV")], tok=True)
                rrep_prev = rrep
                for st_ in range(g * 8, (g + 1) * 8):
                    op("vector", "DV", lambda eng, sems, ss=st_: eng.scalar_tensor_tensor(
                        Y_sb[:, ss * CS:(ss + 1) * CS],
                        Y_sb[:, ss * CS:(ss + 1) * CS],
                        c_nw[:, ss:ss + 1], ps_rep[:, 0:256], ALU.mult, ALU.mult),
                        deps=[rrep])
            prev_m4 = m4_toks[-1]
            y_ready_tok = P.tok("DV")

            if dbg0 and z == 0:
                dma("sync", "XD", t_ysb0[:, :], Y_sb[:, :], deps=[y_ready_tok])
                dma("sync", "XD", t_gmt0[:, :], gmt_sb[:, :], deps=[y_ready_tok])
            yst = dma("sync", "XD",
                      bass.AP(d_yL, z * CS, [[L, 128], [128 * L, 32], [1, CS]]),
                      Y_sb[:, :].rearrange("p (s t) -> p s t", t=CS),
                      deps=[y_ready_tok])
            y_store_toks.append(yst)
            chunk_free = P.tok("XD")

        P.barrier()
        p1.close()

        def hs_set(flag_idx, valcol):
            def fn(eng, sems):
                eng.reg_add(gp_regs["tmp"], gp_regs["myf"], flag_idx)
                return eng.dma_start(
                    out=bass.AP(d_flg, gp_regs["tmp"], [[1, 1], [1, 1]]),
                    in_=c_magic[:1, valcol:valcol + 1], bounds_check=None)
            return fn

        def hs_poll(flag_idx, want_zero=False):
            def fn(eng, sems):
                eng.reg_add(gp_regs["tmp"], gp_regs["otf"], flag_idx)
                eng.reg_mov(gp_regs["cond"], 1)
                with eng.While(gp_regs["cond"]):
                    eng.dma_start(
                        out=c_flagin[:1, :1],
                        in_=bass.AP(d_flg, gp_regs["tmp"], [[1, 1], [1, 1]]),
                        bounds_check=None
                    ).then_inc(sems["HS"], 16)
                    eng.reg_add(gp_regs["hcnt"], gp_regs["hcnt"], 16)
                    eng.wait_ge(sems["HS"], gp_regs["hcnt"])
                    eng.reg_load(gp_regs["fval"], c_flagin[:1, :1])
                    if want_zero:
                        eng.reg_alu(gp_regs["cond"], gp_regs["fval"], 0,
                                    ALU.bitwise_or)
                    else:
                        eng.reg_alu(gp_regs["cond"], gp_regs["fval"], MAGIC,
                                    ALU.bitwise_xor)
                return None
            return fn

        def y_publish(eng, sems):
            return eng.dma_start(
                out=bass.AP(d_yC, gp_regs["yw"], [[1, D * L]]),
                in_=d_yL[:, :].rearrange("a b -> (a b)").rearrange("(o f) -> o f", o=1)
                if False else bass.AP(d_yL, 0, [[1, D * L]]),
                bounds_check=None)
        P.emit("gpsimd", [y_store_toks[-1], P.tok("GD"), gp_ready], y_publish,
               inc="GD", amt=16)
        P.emit("gpsimd", [P.tok("GD")], hs_set(0, 0), inc="GD", amt=16)
        P.emit("gpsimd", [P.tok("GD")], hs_poll(0))
        def y_fetch(eng, sems):
            return eng.dma_start(
                out=bass.AP(d_yL2, 0, [[TH, 2 * D], [1, TH]]),
                in_=bass.AP(d_yC, gp_regs["yr"], [[L, 2 * D], [1, TH]]),
                bounds_check=None)
        P.emit("gpsimd", [], y_fetch, inc="GD", amt=16)
        hs_ready = P.tok("GD")
        gack = P.emit("gpsimd", [hs_ready, P.tok("GD")], hs_set(1, 0),
                      inc="GD", amt=16)

        p2 = ExitStack()
        y_fm = p2.enter_context(nc.sbuf_tensor("y_fm", [128, 64 * 1024], BF16))
        wob = [p2.enter_context(nc.sbuf_tensor(f"wob{i}", [128, 64 * 128], BF16))
               for i in range(2)]
        att_ev = p2.enter_context(nc.sbuf_tensor("att_ev", [128, 512], BF16))
        ps_o = [p2.enter_context(nc.psum_tensor(f"ps_o_{i}", [128, 512], F32))
                for i in range(2)]

        wob_free = [None, None]
        psob_free = [None, None]
        ev_free = None
        last_attn_mm = None
        for half in range(2):
            ylds = []
            for q in range(8):
                ylds.append(dma(
                    "sync", "XD",
                    y_fm[:, q * 8 * 1024:(q + 1) * 8 * 1024]
                        .rearrange("p (b t) -> p b t", t=1024),
                    bass.AP(d_yL2, q * 8 * 128 * TH + half * 1024,
                            [[TH, 128], [128 * TH, 8], [1, 1024]]),
                    deps=[hs_ready, last_attn_mm if half == 1 else None]))
            y_ready = ylds[-1]
            for o in range(32):
                par = o % 2
                wl = dma("gpsimd", "GD",
                         wob[par][:, :].rearrange("p (k m) -> p k m", m=128),
                         i_wout[o, :, :, :].transpose([1, 0, 2]),
                         deps=[wob_free[par], gp_ready])
                for ts_ in range(2):
                    bank = ts_
                    mmt = None
                    for k in range(64):
                        mmt = mm(lambda eng, sems, b=bank, kk=k, t=ts_, p=par: eng.matmul(
                            ps_o[b][:, 0:512],
                            wob[p][:, kk * 128:(kk + 1) * 128],
                            y_fm[:, kk * 1024 + t * 512:kk * 1024 + (t + 1) * 512],
                            start=(kk == 0), stop=(kk == 63)),
                            deps=([wl] + ylds + [psob_free[bank]]) if k == 0 else [],
                            tok=(k == 63))
                    last_attn_mm = mmt
                    ev = op("vector", "DV", lambda eng, sems, b=bank: eng.tensor_copy(
                        att_ev[:, :], ps_o[b][:, 0:512]), deps=[mmt, ev_free])
                    psob_free[bank] = ev
                    st = dma("sync", "XD",
                             d_attnC[o * 128:(o + 1) * 128,
                                     half * 1024 + ts_ * 512:half * 1024 + (ts_ + 1) * 512],
                             att_ev[:, :], deps=[ev])
                    ev_free = st
                wob_free[par] = last_attn_mm
        attn_done = P.tok("XD")
        gcv = gack

        P.barrier()
        p2.close()

        p2b = ExitStack()
        at_tm = p2b.enter_context(nc.sbuf_tensor("at_tm", [128, D], BF16))
        xr_sb = p2b.enter_context(nc.sbuf_tensor("xr_sb", [128, D], F32))
        x1_sb = p2b.enter_context(nc.sbuf_tensor("x1_sb", [128, D], F32))
        bst2 = p2b.enter_context(nc.sbuf_tensor("bst2", [128, 6 * 8], F32))
        st2 = p2b.enter_context(nc.sbuf_tensor("st2", [128, 8], F32))
        x1n_tm = p2b.enter_context(nc.sbuf_tensor("x1n_tm", [128, D], BF16))

        prevst = None
        prevst2 = None
        for tt in range(16):
            tr = dma("sync", "XD", at_tm[:, :], d_attnC[:, tt * 128:(tt + 1) * 128],
                     deps=[attn_done, prevst], transpose=True)
            xr = dma("sync", "XD", xr_sb[:, :], i_xres[tt * 128:(tt + 1) * 128, :],
                     deps=[prevst])
            s2 = ln_stats(lambda ii: at_tm[:, ii * 512:(ii + 1) * 512],
                          bst2, st2, 0, [tr])
            s6 = op("scalar", "AC", lambda eng, sems: eng.activation(
                st2[:, 2:3], st2[:, 1:2], AF.Ln, bias=EPS), deps=[s2])
            s7 = op("scalar", "AC", lambda eng, sems: eng.activation(
                st2[:, 3:4], st2[:, 2:3], AF.Exp, scale=-0.5), deps=[s6])
            s8 = op("vector", "DV", lambda eng, sems: eng.tensor_scalar(
                x1_sb[:, :], at_tm[:, :], st2[:, 0:1], st2[:, 3:4],
                ALU.subtract, ALU.mult), deps=[s7, prevst2])
            s9 = op("vector", "DV", lambda eng, sems: eng.tensor_add(
                x1_sb[:, :], x1_sb[:, :], xr_sb[:, :]), deps=[s8, xr])
            stx1 = dma("sync", "XD", d_x1C[tt * 128:(tt + 1) * 128, :],
                       x1_sb[:, :], deps=[s9])
            u2 = ln_stats(lambda ii: x1_sb[:, ii * 512:(ii + 1) * 512],
                          bst2, st2, 4, [s9])
            u6 = op("scalar", "AC", lambda eng, sems: eng.activation(
                st2[:, 6:7], st2[:, 5:6], AF.Ln, bias=EPS), deps=[u2])
            u7 = op("scalar", "AC", lambda eng, sems: eng.activation(
                st2[:, 7:8], st2[:, 6:7], AF.Exp, scale=-0.5), deps=[u6])
            u8 = op("vector", "DV", lambda eng, sems: eng.tensor_scalar(
                x1n_tm[:, :], x1_sb[:, :], st2[:, 4:5], st2[:, 7:8],
                ALU.subtract, ALU.mult), deps=[u7, stx1])
            stx1n = dma("sync", "XD", d_x1nC[tt * 128:(tt + 1) * 128, :],
                        x1n_tm[:, :], deps=[u8])
            prevst = stx1n
            prevst2 = stx1n
        x1n_done = P.tok("XD")

        P.barrier()
        p2b.close()

        p3 = ExitStack()
        x1n_fm = p3.enter_context(nc.sbuf_tensor("x1n_fm", [128, 32 * TH], BF16))
        wrt_sb = p3.enter_context(nc.sbuf_tensor("wrt_sb", [128, 32 * 16], BF16))
        wa_sb = p3.enter_context(nc.sbuf_tensor("wa_sb", [128, 32 * 64], BF16))
        logit = p3.enter_context(nc.sbuf_tensor("logit", [128, 16], F32))
        tops = p3.enter_context(nc.sbuf_tensor("tops", [128, 8], F32))
        mask1 = p3.enter_context(nc.sbuf_tensor("mask1", [128, 16], F32))
        mask2 = p3.enter_context(nc.sbuf_tensor("mask2", [128, 16], F32))
        comb = p3.enter_context(nc.sbuf_tensor("comb", [128, 16], F32))
        combT = p3.enter_context(nc.sbuf_tensor("combT", [16, 128], F32))
        res_fm = p3.enter_context(nc.sbuf_tensor("res_fm", [64, TH], BF16))
        crep = p3.enter_context(nc.sbuf_tensor("crep", [64, 512], BF16))
        scat = p3.enter_context(nc.sbuf_tensor("scat", [128, 8 * TH], BF16))
        wfb = [p3.enter_context(nc.sbuf_tensor(f"wfb{i}", [128, 32 * 128], BF16))
               for i in range(2)]
        wdb = [p3.enter_context(nc.sbuf_tensor(f"wdb{i}", [128, 8 * 128], BF16))
               for i in range(2)]
        ffn_ev = p3.enter_context(nc.sbuf_tensor("ffn_ev", [128, 512], BF16))
        ps_f = [p3.enter_context(nc.psum_tensor(f"ps_f_{i}", [128, 512], F32))
                for i in range(2)]
        ps_r = p3.enter_context(nc.psum_tensor("ps_r", [128, 512], F32))
        ps_t3 = p3.enter_context(nc.psum_tensor("ps_t3", [128, 512], F32))

        for kb in range(32):
            dma("sync", "XD", x1n_fm[:, kb * TH:(kb + 1) * TH],
                d_x1nC[:, kb * 128:(kb + 1) * 128], deps=[x1n_done],
                transpose=True)
        x1n_ready = P.tok("XD")
        wrt_l = dma("gpsimd", "GD", wrt_sb[:, :].rearrange("p (k m) -> p k m", m=16),
                    i_wrt[:, :, :].transpose([1, 0, 2]), deps=[gp_ready, P.tok("GD")])
        wa_l = dma("gpsimd", "GD", wa_sb[:, :].rearrange("p (k m) -> p k m", m=64),
                   i_wa[:, :, :].transpose([1, 0, 2]), deps=[gp_ready])

        prev = None
        prev_l1 = None
        prev_tc = None
        for tt in range(16):
            mmt = None
            for k in range(32):
                mmt = mm(lambda eng, sems, kk=k, t=tt: eng.matmul(
                    ps_r[0:128, 0:16],
                    x1n_fm[:, kk * TH + t * 128:kk * TH + (t + 1) * 128],
                    wrt_sb[:, kk * 16:(kk + 1) * 16],
                    start=(kk == 0), stop=(kk == 31)),
                    deps=([x1n_ready, wrt_l, prev_l1]) if k == 0 else [],
                    tok=(k == 31))
            l1 = op("vector", "DV", lambda eng, sems: eng.tensor_copy(
                logit[:, :], ps_r[0:128, 0:16]), deps=[mmt])
            prev_l1 = l1
            f1 = op("vector", "DV", lambda eng, sems: eng.reduce_max(
                tops[:, 4:5], logit[:, :], axis=mybir.AxisListType.X), deps=[l1])
            f2 = op("vector", "DV", lambda eng, sems: eng.tensor_scalar_sub(
                logit[:, :], logit[:, :], tops[:, 4:5]), deps=[f1])
            f3 = op("scalar", "AC", lambda eng, sems: eng.activation(
                logit[:, :], logit[:, :], AF.Exp), deps=[f2])
            f4 = op("vector", "DV", lambda eng, sems: eng.reduce_sum(
                tops[:, 5:6], logit[:, :], axis=mybir.AxisListType.X), deps=[f3])
            f5 = op("vector", "DV", lambda eng, sems: eng.reciprocal(
                tops[:, 6:7], tops[:, 5:6]), deps=[f4])
            f6 = op("vector", "DV", lambda eng, sems: eng.tensor_scalar_mul(
                logit[:, :], logit[:, :], tops[:, 6:7]), deps=[f5])
            g1 = op("vector", "DV", lambda eng, sems: eng.reduce_max(
                tops[:, 0:1], logit[:, :], axis=mybir.AxisListType.X), deps=[f6])
            g2 = op("vector", "DV", lambda eng, sems: eng.tensor_scalar(
                mask1[:, :], logit[:, :], tops[:, 0:1], None, ALU.is_ge), deps=[g1])
            g3 = op("vector", "DV", lambda eng, sems: eng.scalar_tensor_tensor(
                mask2[:, :], mask1[:, :], -1e9, logit[:, :], ALU.mult, ALU.add),
                deps=[g2])
            g4 = op("vector", "DV", lambda eng, sems: eng.reduce_max(
                tops[:, 1:2], mask2[:, :], axis=mybir.AxisListType.X), deps=[g3])
            g5 = op("vector", "DV", lambda eng, sems: eng.tensor_scalar(
                mask2[:, :], mask2[:, :], tops[:, 1:2], None, ALU.is_ge), deps=[g4])
            g6 = op("vector", "DV", lambda eng, sems: eng.tensor_add(
                mask1[:, :], mask1[:, :], mask2[:, :]), deps=[g5])
            g7 = op("vector", "DV", lambda eng, sems: eng.tensor_add(
                tops[:, 2:3], tops[:, 0:1], tops[:, 1:2]), deps=[g4])
            g8 = op("vector", "DV", lambda eng, sems: eng.tensor_scalar_add(
                tops[:, 2:3], tops[:, 2:3], 1e-6), deps=[g7])
            g9 = op("vector", "DV", lambda eng, sems: eng.reciprocal(
                tops[:, 3:4], tops[:, 2:3]), deps=[g8])
            h1 = op("vector", "DV", lambda eng, sems: eng.tensor_mul(
                comb[:, :], logit[:, :], mask1[:, :]), deps=[g6, f6])
            h2 = op("vector", "DV", lambda eng, sems: eng.tensor_scalar(
                comb[:, :], comb[:, :], tops[:, 3:4], 0.5, ALU.mult, ALU.mult),
                deps=[h1, g9])
            tc = mm(lambda eng, sems: eng.transpose(
                ps_t3[0:16, 0:128], comb[:, :], c_ident[:, :]),
                deps=[h2, consts_ready, prev_tc], tok=True)
            ec = op("vector", "DV", lambda eng, sems: eng.tensor_copy(
                combT[:, :], ps_t3[0:16, 0:128]), deps=[tc])
            prev_tc = ec
            stc = dma("sync", "XD", d_combC[:, tt * 128:(tt + 1) * 128],
                      combT[:, :], deps=[ec])
            prev = stc
        comb_done = P.tok("XD")

        res_evs = []
        for ts_ in range(4):
            mmt = None
            for k in range(32):
                mmt = mm(lambda eng, sems, kk=k, t=ts_: eng.matmul(
                    ps_r[0:64, 0:512],
                    wa_sb[:, kk * 64:(kk + 1) * 64],
                    x1n_fm[:, kk * TH + t * 512:kk * TH + (t + 1) * 512],
                    start=(kk == 0), stop=(kk == 31)),
                    deps=([x1n_ready, wa_l, prev_l1,
                           res_evs[-1] if res_evs else None]) if k == 0 else [],
                    tok=(k == 31))
            ev = op("vector", "DV", lambda eng, sems, t=ts_: eng.tensor_copy(
                res_fm[:, t * 512:(t + 1) * 512], ps_r[0:64, 0:512]), deps=[mmt])
            res_evs.append(ev)
        res_done = res_evs[-1]

        prevc = None
        for eb in range(8):
            for ei in range(2):
                e = 2 * eb + ei
                for ts_ in range(4):
                    bc = dma("gpsimd", "GD", crep[:, :],
                             bass.AP(d_combC, e * TH + ts_ * 512,
                                     [[0, 64], [1, 512]]),
                             deps=[comb_done, prevc, P.tok("GD")])
                    sc = op("vector", "DV",
                            lambda eng, sems, b=eb, i2=ei, t=ts_: eng.tensor_mul(
                                scat[i2 * 64:(i2 + 1) * 64,
                                     b * TH + t * 512:b * TH + (t + 1) * 512],
                                res_fm[:, t * 512:(t + 1) * 512], crep[:, :]),
                            deps=[bc, res_done])
                    prevc = sc
        scat_done = P.tok("DV")

        wf_free = [None, None]
        psf_free = [None, None]
        evf_free = None
        last_f_mm = None
        for o in range(32):
            par = o % 2
            wl1 = dma("gpsimd", "GD", wfb[par][:, :].rearrange("p (k m) -> p k m", m=128),
                      i_wbase[o, :, :, :].transpose([1, 0, 2]),
                      deps=[wf_free[par], gp_ready])
            wl2 = dma("gpsimd", "GD", wdb[par][:, :].rearrange("p (k m) -> p k m", m=128),
                      i_wbcat[o, :, :, :].transpose([1, 0, 2]),
                      deps=[wf_free[par]])
            for ts_ in range(4):
                bank = ts_ % 2
                mmt = None
                for k in range(32):
                    mmt = mm(lambda eng, sems, b=bank, kk=k, t=ts_, p=par: eng.matmul(
                        ps_f[b][:, 0:512],
                        wfb[p][:, kk * 128:(kk + 1) * 128],
                        x1n_fm[:, kk * TH + t * 512:kk * TH + (t + 1) * 512],
                        start=(kk == 0), stop=False),
                        deps=([wl1, x1n_ready, psf_free[bank]]) if k == 0 else [])
                for kb in range(8):
                    mmt = mm(lambda eng, sems, b=bank, kk=kb, t=ts_, p=par: eng.matmul(
                        ps_f[b][:, 0:512],
                        wdb[p][:, kk * 128:(kk + 1) * 128],
                        scat[:, kk * TH + t * 512:kk * TH + (t + 1) * 512],
                        start=False, stop=(kk == 7)),
                        deps=([wl2, scat_done]) if kb == 0 else [],
                        tok=(kb == 7))
                last_f_mm = mmt
                ev = op("vector", "DV",
                        lambda eng, sems, b=bank, oo=o: eng.tensor_scalar_add(
                            ffn_ev[:, :], ps_f[b][:, 0:512], c_bb[:, oo:oo + 1]),
                        deps=[mmt, evf_free, consts_ready])
                psf_free[bank] = ev
                st = dma("sync", "XD",
                         d_ffnC[o * 128:(o + 1) * 128, ts_ * 512:(ts_ + 1) * 512],
                         ffn_ev[:, :], deps=[ev])
                evf_free = st
            wf_free[par] = last_f_mm
        ffn_done = P.tok("XD")

        P.barrier()
        p3.close()

        p3b = ExitStack()
        fn_tm = p3b.enter_context(nc.sbuf_tensor("fn_tm", [128, D], BF16))
        x1r = p3b.enter_context(nc.sbuf_tensor("x1r", [128, D], F32))
        out_sb = p3b.enter_context(nc.sbuf_tensor("out_sb", [128, D], F32))
        bst3 = p3b.enter_context(nc.sbuf_tensor("bst3", [128, 6 * 8], F32))
        st3 = p3b.enter_context(nc.sbuf_tensor("st3", [128, 8], F32))

        prev = None
        for tt in range(16):
            tr = dma("sync", "XD", fn_tm[:, :], d_ffnC[:, tt * 128:(tt + 1) * 128],
                     deps=[ffn_done, prev], transpose=True)
            xr = dma("sync", "XD", x1r[:, :], d_x1C[tt * 128:(tt + 1) * 128, :],
                     deps=[prev])
            s2 = ln_stats(lambda ii: fn_tm[:, ii * 512:(ii + 1) * 512],
                          bst3, st3, 0, [tr])
            s6 = op("scalar", "AC", lambda eng, sems: eng.activation(
                st3[:, 2:3], st3[:, 1:2], AF.Ln, bias=EPS), deps=[s2])
            s7 = op("scalar", "AC", lambda eng, sems: eng.activation(
                st3[:, 3:4], st3[:, 2:3], AF.Exp, scale=-0.5), deps=[s6])
            s8 = op("vector", "DV", lambda eng, sems: eng.tensor_scalar(
                out_sb[:, :], fn_tm[:, :], st3[:, 0:1], st3[:, 3:4],
                ALU.subtract, ALU.mult), deps=[s7, prev])
            s9 = op("vector", "DV", lambda eng, sems: eng.tensor_add(
                out_sb[:, :], out_sb[:, :], x1r[:, :]), deps=[s8, xr])
            st = dma("sync", "XD", o_out[tt * 128:(tt + 1) * 128, :],
                     out_sb[:, :], deps=[s9])
            prev = st

        for t in taps:
            src = scratch[t]
            ap_src = src[:, :] if len(src.shape) == 2 else src.ap()
            ap_dst = tap_outs[t][:, :] if len(src.shape) == 2 else tap_outs[t].ap()
            dma("sync", "XD", ap_dst, ap_src, deps=[P.tok("XD"), P.tok("GD")])

        P.emit("gpsimd", [gcv], hs_poll(1))
        P.emit("gpsimd", [], hs_set(0, 1), inc="GD", amt=16)
        P.emit("gpsimd", [P.tok("GD")], hs_poll(0, want_zero=True))
        P.emit("gpsimd", [], hs_set(1, 1), inc="GD", amt=16)
        P.emit("gpsimd", [P.tok("GD")], lambda eng, sems: None)

        P.emit("sync", [P.tok("XD")], lambda eng, sems: None)

        p3b.close()
        block = ctx.enter_context(nc.Block())

        @block.sync
        def _(eng):
            for fn in P.ops["sync"]:
                fn(eng, sems)

        @block.gpsimd
        def _(eng):
            for fn in P.ops["gpsimd"]:
                fn(eng, sems)

        @block.vector
        def _(eng):
            for fn in P.ops["vector"]:
                fn(eng, sems)

        @block.scalar
        def _(eng):
            for fn in P.ops["scalar"]:
                fn(eng, sems)

        @block.tensor
        def _(eng):
            for fn in P.ops["tensor"]:
                fn(eng, sems)

    return nc



def _host_inputs(inputs):
    hs = np.asarray(inputs["L0_attn_hidden_states"], dtype=np.float32)
    w_in = np.asarray(inputs["L0_attn_in_proj_w"], dtype=np.float32)
    conv_w = np.asarray(inputs["L0_attn_conv_w"], dtype=np.float32)
    conv_b = np.asarray(inputs["L0_attn_conv_b"], dtype=np.float32)
    dt_bias = np.asarray(inputs["L0_attn_dt_bias"], dtype=np.float32)
    A_log = np.asarray(inputs["L0_attn_A_log"], dtype=np.float32)
    Dv = np.asarray(inputs["L0_attn_D"], dtype=np.float32)
    norm_w = np.asarray(inputs["L0_attn_norm_w"], dtype=np.float32)
    w_out = np.asarray(inputs["L0_attn_out_proj_w"], dtype=np.float32)
    base_w = np.asarray(inputs["L0_ffn_base_w"], dtype=np.float32)
    base_b = np.asarray(inputs["L0_ffn_base_b"], dtype=np.float32)
    Am = np.asarray(inputs["L0_ffn_A"], dtype=np.float32)
    Bm = np.asarray(inputs["L0_ffn_B"], dtype=np.float32)
    router_w = np.asarray(inputs["L0_ffn_router_w"], dtype=np.float32)

    w_inT = np.ascontiguousarray(w_in.T)
    w_outT = np.ascontiguousarray(w_out.T)
    w_baseT = np.ascontiguousarray(base_w.T)
    bcatT = np.ascontiguousarray(Bm.transpose(0, 2, 1).reshape(1024, 4096))
    w_aT = np.ascontiguousarray(Am.T)
    w_rtT = np.ascontiguousarray(router_w.T)

    diag = np.zeros((2, 128, 16 * CS), dtype=np.float32)
    jj = np.arange(128)
    for jt in range(2):
        for hh in range(16):
            diag[jt, jj, hh * CS + jj + jt * 128] = 1.0
    diag = diag.astype(ml_dtypes.bfloat16)

    ident = np.eye(128, dtype=np.float32)
    ones_bf = np.ones((128, 1), dtype=ml_dtypes.bfloat16)
    ones_f = np.ones((1, 128), dtype=np.float32)

    w_out_blocks = np.ascontiguousarray(
        w_outT.reshape(64, 128, 32, 128).transpose(2, 0, 1, 3))
    w_base_blocks = np.ascontiguousarray(
        w_baseT.reshape(32, 128, 32, 128).transpose(2, 0, 1, 3))
    w_bcat_blocks = np.ascontiguousarray(
        bcatT.reshape(8, 128, 32, 128).transpose(2, 0, 1, 3))
    w_a_blocks = np.ascontiguousarray(w_aT.reshape(32, 128, 64))
    w_rt_blocks = np.ascontiguousarray(w_rtT.reshape(32, 128, 16))

    in_maps = []
    for c in range(NCORES):
        b, H = divmod(c, 2)
        cols = [8192 + 4096 * H + 128 * s for s in range(32)]
        cols += [16384 + 512 * H + 128 * s for s in range(4)]
        cols += [17408 + 512 * H + 128 * s for s in range(4)]
        cols += [4096 * H + 128 * s for s in range(32)]
        w_strips = np.ascontiguousarray(
            np.stack([w_inT[:, c0:c0 + 128] for c0 in cols], axis=0))
        w_dt = np.ascontiguousarray(w_inT[:, 18432 + 64 * H:18432 + 64 * H + 64])

        chans = [4096 * H + 128 * s for s in range(32)]
        chans += [8192 + 512 * H + 128 * s for s in range(4)]
        chans += [9216 + 512 * H + 128 * s for s in range(4)]
        cw = np.ascontiguousarray(
            np.stack([conv_w[c0:c0 + 128, 0, :] for c0 in chans], axis=0)
            .transpose(1, 0, 2).reshape(128, 160))
        cb = np.ascontiguousarray(
            np.stack([conv_b[c0:c0 + 128] for c0 in chans], axis=0).T)

        d_exp = np.ascontiguousarray(
            np.repeat(Dv[64 * H:64 * H + 64], 64).reshape(32, 128).T)
        nw = np.ascontiguousarray(
            norm_w[4096 * H:4096 * H + 4096].reshape(32, 128).T)

        hsk = np.array([[H * 32 * 128 * L, H * TH, H * 8, (1 - H) * 8,
                         0, 0, 0, 0]], dtype=np.int32)

        in_maps.append({
            "x_full": np.ascontiguousarray(hs[b]),
            "x_res": np.ascontiguousarray(hs[b, H * TH:(H + 1) * TH]),
            "w_strips": w_strips,
            "w_dt": w_dt,
            "conv_w": cw,
            "conv_b": cb,
            "dt_bias": np.ascontiguousarray(dt_bias[64 * H:64 * H + 64, None]),
            "a_log": np.ascontiguousarray(A_log[64 * H:64 * H + 64, None]),
            "d_exp": d_exp,
            "norm_w": nw,
            "w_out": w_out_blocks,
            "w_base": w_base_blocks,
            "base_b": np.ascontiguousarray(base_b.reshape(32, 128).T),
            "w_bcat": w_bcat_blocks,
            "w_a": w_a_blocks,
            "w_router": w_rt_blocks,
            "diag_inj": diag,
            "ident": ident,
            "ones_bf": ones_bf,
            "ones_f": ones_f,
            "hsk": hsk,
        })
    return in_maps


def kernel(**inputs):
    in_maps = _host_inputs(inputs)
    nc = build_program(taps=DEBUG_TAPS)
    res = run_bass_kernel_spmd(nc, in_maps, core_ids=list(range(NCORES)))
    out = np.empty((4, L, D), dtype=np.float32)
    for c in range(NCORES):
        b, H = divmod(c, 2)
        out[b, H * TH:(H + 1) * TH] = np.asarray(res.results[c]["o_out"])
    if DEBUG_TAPS:
        kernel.last_results = res
    return out
